# version 03 (final)
# baseline (speedup 1.0000x reference)
import numpy as np
import ml_dtypes
import concourse.bass as bass
import concourse.mybir as mybir
from concourse.bass_utils import run_bass_kernel_spmd

F32 = mybir.dt.float32
BF16 = mybir.dt.bfloat16
AF = mybir.ActivationFunctionType
ALU = mybir.AluOpType
AX = mybir.AxisListType
BT = ml_dtypes.bfloat16

T = 2048
DM = 1024
NCH = 8
NTT = 4
TP = T + 16
PAD = 16
NEG = -1e30
NHEAD = 16
FHEAD = 8
WIN = 512

DEBUG_TAPS = []
L0A_LIMIT = "full"
L1F_LIMIT = "full"
L0F_LAYERS = 2
L0F_LIMIT = "full"
DBG_B = False
DETECT_RACES = False


def _host_consts():
    c = {}
    c["eye"] = np.eye(128).astype(BT)
    c["eyef"] = np.eye(128, dtype=np.float32)
    c["ones_row"] = np.ones((1, 128), np.float32).astype(BT)
    c["nones_row"] = (-np.ones((1, 128), np.float32)).astype(BT)
    c["mean_col"] = np.full((128, 1), 1.0 / DM, np.float32).astype(BT)
    c["ones_col"] = np.ones((128, 1), np.float32).astype(BT)
    c["eps_col"] = np.full((128, 1), 1e-30, np.float32)
    bo = np.zeros((128, 33), np.float32)
    bo[0:64, 0] = 1.0
    bo[64:128, 32] = 1.0
    c["half_ones"] = bo.astype(BT)

    DHE = 32
    inv_freq = 1.0 / 10000.0 ** (np.arange(0, DHE, 2, dtype=np.float64) / DHE)
    freqs = np.arange(T, dtype=np.float64)[:, None] * inv_freq[None, :]
    emb = np.concatenate([freqs, freqs], axis=-1)
    cos = np.cos(emb).T
    sin = np.sin(emb).T
    t1 = np.zeros((128, T), np.float64)
    t2 = np.zeros((128, T), np.float64)
    t1[0:32] = 1.0
    t1[64:96] = 1.0
    t1[32:64] = cos
    t1[96:128] = cos
    sgn = np.ones((32, 1))
    sgn[0::2] = -1.0
    t2[32:64] = sin * sgn
    t2[96:128] = sin * sgn
    c["l0a_t1"] = t1.astype(BT)
    c["l0a_t2"] = t2.astype(BT)

    DH, QKS = 64, 8.0
    half_idx = np.arange(0, DH, 2, dtype=np.float64)
    inv_f = 1.0 / 10000.0 ** (half_idx / DH)
    tt = np.arange(T, dtype=np.float64)
    fr = tt[:, None] * inv_f[None, :]
    fr = np.concatenate([fr, fr], axis=-1)
    base = (half_idx + 0.4 * DH) / (1.4 * DH)
    power = (tt - T // 2) / float(WIN // 2)
    scale = base[None, :] ** power[:, None]
    scale = np.concatenate([scale, scale], axis=-1)
    cosf, sinf = np.cos(fr), np.sin(fr)
    sg = np.ones((DH, 1))
    sg[0:32] = -1.0
    q1 = (QKS * cosf * scale).T
    q2 = (QKS * sinf * scale).T * sg
    k1 = (cosf / scale).T
    k2 = (sinf / scale).T * sg
    c["xp_q1"] = np.concatenate([q1, q1], 0).astype(np.float32)
    c["xp_q2"] = np.concatenate([q2, q2], 0).astype(np.float32)
    c["xp_k1"] = np.concatenate([k1, k1], 0).astype(np.float32)
    c["xp_k2"] = np.concatenate([k2, k2], 0).astype(np.float32)

    p = np.arange(128)[:, None]
    y = np.arange(896)[None, :]
    c["mask_causal"] = np.where(y >= p + 384, 0.0, NEG).astype(np.float32).astype(BT)
    y2 = np.arange(1024)[None, :]
    c["mask_low"] = np.where(y2 - 512 <= p, 0.0, NEG).astype(np.float32).astype(BT)
    y3 = np.arange(896)[None, :]
    c["mask_up"] = np.where(p <= y3 - 384, 0.0, NEG).astype(np.float32).astype(BT)
    return c


def _col_layout(v, nch):
    out = np.zeros((128, nch), np.float32)
    full = np.zeros(128 * nch, np.float32)
    full[:v.shape[0]] = v
    for ch in range(nch):
        out[:, ch] = full[ch * 128:(ch + 1) * 128]
    return out


def _host_prep(inputs):
    w = {}
    f32 = lambda a: np.ascontiguousarray(np.asarray(a), dtype=np.float32)
    w["l0a_wq"] = f32(inputs["L0_attn_Wq_down"])
    w["l0a_wk"] = f32(inputs["L0_attn_Wk_down"])
    w["l0a_wv"] = f32(inputs["L0_attn_Wv_down"])
    w["l0a_wc"] = f32(inputs["L0_attn_Wc"])
    w["l0a_qc"] = f32(inputs["L0_attn_Wq_up_c"])
    w["l0a_qe"] = f32(inputs["L0_attn_Wq_up_e"])
    w["l0a_kc"] = f32(inputs["L0_attn_Wk_up_c"])
    w["l0a_ke"] = f32(inputs["L0_attn_Wk_up_e"])
    perm = np.arange(32).reshape(16, 2)[:, ::-1].reshape(32)
    w["l0a_qes"] = f32(inputs["L0_attn_Wq_up_e"])[:, perm].copy()
    w["l0a_kes"] = f32(inputs["L0_attn_Wk_up_e"])[:, perm].copy()
    w["l0a_wvu"] = f32(inputs["L0_attn_Wv_up"])

    permh = np.concatenate([np.arange(32, 64), np.arange(0, 32)])
    wqkv = f32(inputs["L0_ffn_wqkv"])
    w["l0f_wqkv"] = wqkv
    sw = wqkv[:, :, :1024].reshape(2, 1024, 16, 64)[:, :, :, permh]
    w["l0f_wqkv_sw"] = np.ascontiguousarray(sw.reshape(2, 1024, 1024))
    w["l0f_wo"] = f32(inputs["L0_ffn_wo"])
    w["l0f_w1"] = f32(inputs["L0_ffn_w1"])
    w["l0f_w2"] = f32(inputs["L0_ffn_w2"])
    b1f = f32(inputs["L0_ffn_b1"])
    w["l0f_b1ac"] = np.stack([_col_layout(b1f[l][:2730], 22) for l in range(2)])
    w["l0f_b1gc"] = np.stack([_col_layout(b1f[l][2730:], 22) for l in range(2)])
    w["l0f_b2c"] = np.stack([_col_layout(f32(inputs["L0_ffn_b2"])[l], NCH) for l in range(2)])
    w["l0f_gc"] = np.stack([_col_layout(f32(inputs["L0_ffn_ln_g"])[l], NCH) for l in range(2)])
    w["l0f_bc"] = np.stack([_col_layout(f32(inputs["L0_ffn_ln_b"])[l], NCH) for l in range(2)])
    qs = f32(inputs["L0_ffn_q_scale"])
    ks = f32(inputs["L0_ffn_k_scale"])
    permh2 = permh
    w["l0f_qsc"] = np.stack([np.concatenate([qs[l], qs[l]])[:, None] for l in range(2)])
    w["l0f_qsc_sw"] = np.stack([np.concatenate([qs[l][permh2], qs[l][permh2]])[:, None] for l in range(2)])
    w["l0f_ksc"] = np.stack([np.concatenate([ks[l], ks[l]])[:, None] for l in range(2)])
    w["l0f_ksc_sw"] = np.stack([np.concatenate([ks[l][permh2], ks[l][permh2]])[:, None] for l in range(2)])

    w["l1a_wq"] = np.ascontiguousarray(f32(inputs["L1_attn_Wq"]).T)
    w["l1a_wk"] = np.ascontiguousarray(f32(inputs["L1_attn_Wk"]).T)
    w["l1a_wv"] = np.ascontiguousarray(f32(inputs["L1_attn_Wv"]).T)
    w["l1a_wo"] = np.ascontiguousarray(f32(inputs["L1_attn_Wo"]).T)
    w["l1a_boc"] = _col_layout(f32(inputs["L1_attn_bo"]), NCH)

    w["l1f_gc"] = _col_layout(f32(inputs["L1_ffn_ln_g"]), NCH)
    w["l1f_bc"] = _col_layout(f32(inputs["L1_ffn_ln_b"]), NCH)
    w["l1f_aw"] = f32(inputs["L1_ffn_attn_w"])
    w["l1f_abc"] = _col_layout(f32(inputs["L1_ffn_attn_b"]), NCH)
    w["l1f_nw"] = f32(inputs["L1_ffn_next_w"])
    w["l1f_nbc"] = _col_layout(f32(inputs["L1_ffn_next_b"]), NCH)
    w["l1f_w1"] = f32(inputs["L1_ffn_w1"])
    w["l1f_w2"] = f32(inputs["L1_ffn_w2"])
    w["l1f_b1"] = f32(inputs["L1_ffn_b1"])
    w["l1f_b2"] = f32(inputs["L1_ffn_b2"])
    return w


ENGINES = ("sync", "tensor", "vector", "scalar", "gpsimd")


class Plan:
    def __init__(self):
        self.ops = {e: [] for e in ENGINES}
        self.cnt = {}

    def op(self, eng, fn, waits=(), inc=None, inc_by=1):
        rec = {"fn": fn, "waits": tuple(w for w in waits if w is not None),
               "inc": None}
        if inc is not None:
            self.cnt[inc] = self.cnt.get(inc, 0) + inc_by
            rec["inc"] = (inc, inc_by)
        self.ops[eng].append(rec)
        return self.cnt.get(inc, 0) if inc is not None else None

    def val(self, sem):
        return self.cnt.get(sem, 0)

    def barrier(self, engines=ENGINES):
        vals = {}
        dmaw = tuple((s, self.val(s)) for s in list(self.cnt)
                     if s.startswith("d"))
        for e in engines:
            vals[e] = self.op(e, lambda en: en.engine_nop()
                              if hasattr(en, "engine_nop") else en.nop(),
                              waits=dmaw, inc="B_" + e)
        for e in engines:
            waits = tuple(("B_" + o, vals[o]) for o in engines if o != e)
            self.op(e, lambda en: None, waits=waits)

    def emit(self, nc):
        from contextlib import ExitStack
        with ExitStack() as st:
            sems = {name: st.enter_context(nc.semaphore(name)) for name in self.cnt}
            blk = st.enter_context(nc.Block())
            for eng_name in ENGINES:
                ops = self.ops[eng_name]
                if not ops:
                    continue

                def runner(e, ops=ops, sems=sems):
                    for rec in ops:
                        for (s, v) in rec["waits"]:
                            if s in sems and v > 0:
                                e.wait_ge(sems[s], v)
                        ins = rec["fn"](e)
                        if rec["inc"] is not None:
                            s, n = rec["inc"]
                            if ins is None:
                                ins = (e.engine_nop()
                                       if hasattr(e, "engine_nop") else e.nop())
                            ins.then_inc(sems[s], n)
                getattr(blk, eng_name)(runner)


class Ctx:
    pass


def build_ctx(nc, stack):
    ctx = Ctx()
    ctx.nc = nc
    ctx.stack = stack
    ctx.plan = Plan()
    E = stack.enter_context

    hc = _host_consts()
    ctx.cap = {k: nc.inline_tensor(np.ascontiguousarray(v), "c_" + k).ap()
               for k, v in hc.items()}

    D = {}
    D["x"] = nc.dram_tensor("x", [2, T, DM], F32, kind="ExternalInput").ap()
    D["out"] = nc.dram_tensor("out", [2, T, DM], F32, kind="ExternalOutput").ap()
    D["r_spill"] = nc.dram_tensor("r_spill", [128, NCH * T], F32, kind="Internal").ap()
    ctx.D = D

    def din(name, shape):
        if name not in D:
            D[name] = nc.dram_tensor(name, list(shape), F32, kind="ExternalInput").ap()
        return D[name]
    ctx.din = din

    S = {}
    S["R"] = E(nc.sbuf_tensor("R", [128, NCH * T], F32))
    S["UB"] = E(nc.sbuf_tensor("UB", [128, NCH * TP], BF16))
    S["QY"] = E(nc.sbuf_tensor("QY", [128, NCH * T], BF16))
    S["VH"] = E(nc.sbuf_tensor("VH", [128, 4 * 1040], BF16))
    S["WA"] = E(nc.sbuf_tensor("WA", [128, 8 * 1024], BF16))
    S["WS"] = E(nc.sbuf_tensor("WS", [128, 2048], F32))
    S["EB"] = E(nc.sbuf_tensor("EB", [128, 1024], BF16))
    S["TMP"] = E(nc.sbuf_tensor("TMP", [128, 2048], F32))
    S["TMP2"] = E(nc.sbuf_tensor("TMP2", [128, 512], F32))
    S["BCA"] = E(nc.sbuf_tensor("BCA", [128, 2 * 2064], BF16))
    S["ROWB"] = E(nc.sbuf_tensor("ROWB", [128, 4096], BF16))
    S["ROWF"] = E(nc.sbuf_tensor("ROWF", [128, 2048], F32))
    S["COLS"] = E(nc.sbuf_tensor("COLS", [128, 96], F32))
    S["XT"] = E(nc.sbuf_tensor("XT", [128, 8192], BF16))
    S["MSK"] = E(nc.sbuf_tensor("MSK", [128, 896 + 1024 + 896], BF16))
    S["CST"] = E(nc.sbuf_tensor("CST", [128, 512], BF16))
    S["CSTF"] = E(nc.sbuf_tensor("CSTF", [128, 128], F32))
    S["CSTE"] = E(nc.sbuf_tensor("CSTE", [128, 2], F32))
    S["KQB"] = E(nc.sbuf_tensor("KQB", [128, 1536], BF16))
    rb = S["R"][:, :].bitcast(BF16)
    S["KC"] = rb[:, 0:NCH * T]
    S["LAT"] = rb[:, NCH * T:NCH * T + 2048]
    S["ROWS"] = S["TMP"]
    ctx.S = S
    ctx.P = [E(nc.psum_tensor(f"p{i}", [128, 512], F32)) for i in range(8)]
    ctx.uid = 0
    ctx.ws_last = {0: 0, 1: 0}
    ctx.sb_last = [0, 0]
    ctx.eb_last = [0, 0]
    ctx.av_last_V = 0
    ctx.rzb_last_V = 0
    ctx.vh_last = [0, 0, 0, 0]
    ctx.p67_free = [0, 0]
    ctx.w1_slot_free = [0, 0]
    ctx.w2_slot_free = [0] * 8
    ctx.qy_read_T = 0
    ctx.t2_free = 0
    ctx.t2_free2 = [0, 0]
    ctx.eb_qk_free = [0, 0]
    ctx.p2_free = 0
    ctx.p3_free = 0
    ctx.rowf_free = 0
    ctx.ybank_free = [0] * 8
    ctx.xt_free_T = 0
    ctx.qy_rd = 0
    ctx.ybank_free_S = 0
    ctx.kqb_free = 0
    ctx.t2b_free = 0
    ctx.bcolA_free = 0
    ctx.mhat_v = 0
    ctx.z8_free = 0
    ctx.p7_free = 0
    ctx.browb_free = 0
    ctx.mkt_free = 0
    ctx.mkt_free_G = 0
    ctx.head_mk_last_T = 0
    ctx.brow_b_cur = 0
    ctx.ybank_free_S = 0
    ctx.last_av_T = 0
    ctx.r_spilled = 0
    ctx.r_ready_T = 0
    ctx.r_ready_V = 0
    ctx.r_ready_dG = 0
    return ctx


def _u(ctx):
    ctx.uid += 1
    return ctx.uid


def lp(ctx, fn):
    def wrapped(e, fn=fn):
        with ctx.nc.allow_low_precision("intentional bf16 intermediate"):
            return fn(e)
    return wrapped


def dma(ctx, dst, src, sem="dS", eng="sync", waits=()):
    return ctx.plan.op(eng, lambda e, d=dst, s=src: e.dma_start(out=d, in_=s),
                       waits=waits, inc=sem, inc_by=16)


def load_consts(ctx):
    p, S, cap = ctx.plan, ctx.S, ctx.cap
    CST, MSK = S["CST"], S["MSK"]
    dma(ctx, CST[:, 0:128], cap["eye"])
    dma(ctx, ctx.S["CSTF"][:, 0:128], cap["eyef"])
    dma(ctx, CST[:, 128:129], cap["mean_col"])
    dma(ctx, CST[:, 129:162], cap["half_ones"])
    dma(ctx, CST[0:1, 162:290], cap["ones_row"])
    dma(ctx, CST[64:65, 162:290], cap["ones_row"])
    dma(ctx, CST[:, 418:419], cap["ones_col"])
    dma(ctx, ctx.S["CSTE"][:, 0:1], cap["eps_col"])
    dma(ctx, CST[0:1, 290:418], cap["nones_row"])
    dma(ctx, CST[64:65, 290:418], cap["nones_row"])
    ctx.plan.op("vector", lambda e: e.memset(ctx.S["BCA"][:, :], 0.0), inc="V")
    for c_ in range(NCH):
        ctx.plan.op("vector", lambda e, c_=c_: e.memset(
            ctx.S["UB"][:, c_ * TP:c_ * TP + PAD], 0.0), inc="V")
    dma(ctx, MSK[:, 0:896], cap["mask_causal"])
    dma(ctx, MSK[:, 896:1920], cap["mask_low"])
    dma(ctx, MSK[:, 1920:2816], cap["mask_up"])
    return p.val("dS")


def ub_ap(ctx, c, t0, n, shift=False):
    off = c * TP + PAD + t0 - (1 if shift else 0)
    return ctx.S["UB"][:, off:off + n]


def load_matrix(ctx, src_ap, nkc, ncols, dst, dst_off=0, scale=None,
                dst_free_T=0):
    p, S = ctx.plan, ctx.S
    vals = []
    nrows = src_ap.shape[0]
    for kc in range(nkc):
        par = kc % 2
        wo = par * 1024
        rc = min(128, nrows - kc * 128)
        dsem = "dW0" if par == 0 else "dW1"
        dv = ctx.plan.op(
            "sync",
            lambda e, s=src_ap, k=kc, wo=wo, nc_=ncols, rc=rc:
                e.dma_start(out=S["WS"][0:rc, wo:wo + nc_],
                            in_=s[k * 128:k * 128 + rc, 0:nc_]),
            waits=(("Wk", ctx.ws_last[par]),), inc=dsem, inc_by=16)
        do = dst_off + kc * ncols
        if scale is None:
            fn = lambda e, wo=wo, do=do, n=ncols, d=dst, rc=rc: \
                e.tensor_copy(out=d[0:rc, do:do + n], in_=S["WS"][0:rc, wo:wo + n])
        else:
            fn = lambda e, wo=wo, do=do, n=ncols, d=dst, sc=scale, rc=rc: \
                e.tensor_scalar_mul(out=d[0:rc, do:do + n],
                                    in0=S["WS"][0:rc, wo:wo + n], scalar1=sc)
        cv = p.op("gpsimd", fn, waits=((dsem, dv), ("T", dst_free_T)), inc="Wk")
        ctx.ws_last[par] = cv
        vals.append(cv)
    return vals


def emit_ln(ctx, src="R", dst="UB", gcol=None, bcol=None, shifted=False,
            sq_area="QY", col_dv=0, also_f32=False, spill_inside=False):
    p, S, P = ctx.plan, ctx.S, ctx.P
    p.barrier()
    UB, QY, BCA, ROWS, ROWB, CST = (S[k] for k in
        ("UB", "QY", "BCA", "ROWS", "ROWB", "CST"))
    SQB = S[sq_area]
    mean_col = CST[:, 128:129]
    ones_row = CST[0:1, 162:290]

    cast_done = [0] * NCH
    if src == "R":
        for c in range(NCH):
            eng = "vector" if c % 2 == 0 else "gpsimd"
            sem = "V" if eng == "vector" else "G"
            p.op(eng, lambda e, c=c: e.memset(S["UB"][:, c * TP:c * TP + PAD], 0.0))
            cast_done[c] = p.op(
                eng, lambda e, c=c: e.tensor_copy(
                    out=S["UB"][:, c * TP + PAD:c * TP + PAD + T],
                    in_=S["R"][:, c * T:(c + 1) * T]),
                waits=(("dG", ctx.r_ready_dG),), inc=sem)
        src_ap = lambda c, t0, n: ub_ap(ctx, c, t0, n, shift=(shifted and c >= 4))
    else:
        src_ap = lambda c, t0, n: ub_ap(ctx, c, t0, n, shift=(shifted and c >= 4))

    sq_done = [0] * NCH
    mm_last = 0
    for c in range(NCH):
        par = c % 2
        eng = "vector" if c % 2 == 0 else "gpsimd"
        sem = "V" if eng == "vector" else "G"
        waits = [("T", mm_last if c >= 2 else 0)]
        if src == "R":
            waits.append(("V" if c % 2 == 0 else "G", cast_done[c]))
        sq_done[c] = p.op(
            eng, lambda e, c=c, par=par: e.tensor_mul(
                out=SQB[:, par * 2048:par * 2048 + T],
                in0=src_ap(c, 0, T), in1=src_ap(c, 0, T)),
            waits=waits, inc=sem)
        for tt in range(NTT):
            p.op("tensor",
                 lambda e, c=c, tt=tt: e.matmul(
                     out=P[tt][0:1, :], lhsT=mean_col,
                     rhs=src_ap(c, tt * 512, 512),
                     start=(c == 0), stop=(c == NCH - 1)),
                 waits=(("V" if c % 2 == 0 else "G", cast_done[c] if src == "R" else 0),))
            mm_last = p.op("tensor",
                 lambda e, c=c, tt=tt, par=par: e.matmul(
                     out=P[4 + tt][0:1, :], lhsT=mean_col,
                     rhs=SQB[:, par * 2048 + tt * 512:par * 2048 + (tt + 1) * 512],
                     start=(c == 0), stop=(c == NCH - 1)),
                 waits=(("V" if c % 2 == 0 else "G", sq_done[c]),), inc="T")

    ROWF = S["ROWF"]
    ev = 0
    for tt in range(NTT):
        p.op("vector", lambda e, tt=tt: e.tensor_copy(
            out=ROWS[0:1, tt * 512:(tt + 1) * 512], in_=P[tt][0:1, :]),
            waits=(("T", mm_last),))
        ev = p.op("vector", lambda e, tt=tt: e.tensor_copy(
            out=ROWF[0:1, tt * 512:(tt + 1) * 512], in_=P[4 + tt][0:1, :]),
            inc="V")
    v2 = p.op("vector", lambda e: e.tensor_copy(out=ROWB[0:1, 0:T],
                                                in_=ROWS[0:1, 0:T]),
              waits=(("V", ev),), inc="V")
    v1 = 0
    for tt in range(NTT):
        p.op("vector", lambda e, tt=tt: e.tensor_mul(
            out=S["TMP2"][0:1, 0:512], in0=ROWS[0:1, tt * 512:(tt + 1) * 512],
            in1=ROWS[0:1, tt * 512:(tt + 1) * 512]), waits=(("V", ev),))
        v1 = p.op("vector", lambda e, tt=tt: e.scalar_tensor_tensor(
            out=ROWF[0:1, tt * 512:(tt + 1) * 512],
            in0=ROWF[0:1, tt * 512:(tt + 1) * 512], scalar=1e-5,
            in1=S["TMP2"][0:1, 0:512], op0=ALU.add, op1=ALU.subtract), inc="V")
    s1 = p.op("scalar", lambda e: e.activation(out=ROWS[0:1, 0:T],
                                               in_=ROWF[0:1, 0:T],
                                               func=AF.Ln),
              waits=(("V", v1), ("V", v2)))
    s2 = p.op("scalar", lambda e: e.activation(out=ROWB[0:1, T:2 * T],
                                               in_=ROWS[0:1, 0:T],
                                               func=AF.Exp, scale=-0.5), inc="S")

    bc_done = 0
    for i, (src_row, dsto) in enumerate(((ROWB[0:1, 0:T], 0), (ROWB[0:1, T:2 * T], 2064))):
        for tt in range(NTT):
            tv = p.op("tensor", lambda e, sr=src_row, tt=tt: e.matmul(
                out=P[6 + tt % 2][:, :], lhsT=ones_row,
                rhs=sr[0:1, tt * 512:(tt + 1) * 512], start=True, stop=True),
                waits=(("S", s2), ("V", v2 if i == 0 else 0),
                       ("V", bc_done)), inc="T")
            bc_done = p.op("vector", lambda e, tt=tt, dsto=dsto: e.tensor_copy(
                out=BCA[:, dsto + tt * 512:dsto + (tt + 1) * 512],
                in_=P[6 + tt % 2][:, :]), waits=(("T", tv),), inc="V")

    if spill_inside:
        emit_spill_R(ctx)
    ctx.ln_done = {"V": 0, "G": 0}
    for c in range(NCH):
        eng = "vector" if c % 2 == 0 else "gpsimd"
        sem = "V" if eng == "vector" else "G"
        sh = 1 if (shifted and c >= 4) else 0
        if dst == "UB":
            d_ap = ub_ap(ctx, c, 0, T)
        else:
            d_ap = QY[:, c * T:(c + 1) * T]
        s_ap = ub_ap(ctx, c, 0, T) if src == "R" or src == "UB" else None
        waits = [("V", bc_done), ("T", mm_last)]
        p.op(eng, lambda e, d=d_ap, s=s_ap, sh=sh: e.tensor_tensor(
            out=d, in0=s, in1=BCA[:, sh:sh + T], op=ALU.subtract), waits=waits)
        p.op(eng, lambda e, d=d_ap, sh=sh: e.tensor_tensor(
            out=d, in0=d, in1=BCA[:, 2064 + sh:2064 + sh + T], op=ALU.mult))
        if gcol is not None:
            ctx.ln_done[sem] = p.op(eng, lambda e, d=d_ap, c=c: e.tensor_scalar(
                out=d, in0=d, scalar1=gcol[:, c:c + 1], scalar2=bcol[:, c:c + 1],
                op0=ALU.mult, op1=ALU.add), waits=(("dC", col_dv),), inc=sem)
        else:
            ctx.ln_done[sem] = p.op(eng, lambda e: None, inc=sem)
        if also_f32:
            rsl = S["R"][:, c * T:(c + 1) * T]
            p.op(eng, lambda e, r=rsl, sh=sh: e.tensor_tensor(
                out=r, in0=r, in1=BCA[:, 0:T], op=ALU.subtract),
                waits=(("dG", ctx.r_spilled),))
            ctx.ln_done[sem] = p.op(eng, lambda e, r=rsl, sh=sh: e.tensor_tensor(
                out=r, in0=r, in1=BCA[:, 2064:2064 + T], op=ALU.mult), inc=sem)
    if shifted:
        mv = p.op("vector", lambda e: e.scalar_tensor_tensor(
            out=S["TMP2"][:, 0:1], in0=BCA[:, 0:1], scalar=-1.0,
            in1=BCA[:, 2064:2065], op0=ALU.mult, op1=ALU.mult),
            waits=(("V", bc_done),), inc="V")
        pv = 0
        for c in range(4, NCH):
            if gcol is not None:
                pv = p.op("gpsimd", lambda e, c=c: e.tensor_scalar(
                    out=S["UB"][:, c * TP + PAD - 1:c * TP + PAD],
                    in0=S["TMP2"][:, 0:1], scalar1=gcol[:, c:c + 1],
                    scalar2=bcol[:, c:c + 1], op0=ALU.mult, op1=ALU.add),
                    waits=(("V", mv), ("dC", col_dv)) if c == 4 else (),
                    inc="G")
            else:
                pv = p.op("gpsimd", lambda e, c=c: e.tensor_copy(
                    out=S["UB"][:, c * TP + PAD - 1:c * TP + PAD],
                    in_=S["TMP2"][:, 0:1]),
                    waits=(("V", mv),) if c == 4 else (), inc="G")
        ctx.ln_done["G"] = max(ctx.ln_done["G"], pv)
    return ctx.ln_done


def src_ap_full(ctx, src_ap, c):
    return src_ap(c, 0, T)


def emit_spill_R(ctx):
    v = dma(ctx, ctx.D["r_spill"], ctx.S["R"][:, :], sem="dG", eng="sync",
            waits=(("T", ctx.plan.val("T")), ("V", ctx.plan.val("V")),
                   ("G", ctx.plan.val("G"))))
    ctx.r_spilled = v
    return v


def emit_reload_R(ctx, extra_waits=()):
    p = ctx.plan
    v = dma(ctx, ctx.S["R"][:, :], ctx.D["r_spill"], sem="dG", eng="sync",
            waits=tuple(extra_waits) + (("dG", ctx.r_spilled),
                                        ("T", p.val("T")), ("V", p.val("V")),
                                        ("G", p.val("G")), ("S", p.val("S"))))
    ctx.r_ready_dG = v
    return v


def emit_input_load(ctx, seq):
    p, S, P = ctx.plan, ctx.S, ctx.P
    p.barrier()
    eyef = S["CSTF"]
    ws_free = [0, 0]
    ev_done = {}
    for tc in range(16):
        par = tc % 2
        wo = par * 1024
        dsem = "dX0" if par == 0 else "dX1"
        dv = dma(ctx, S["WS"][:, wo:wo + 1024],
                 ctx.D["x"][seq, tc * 128:(tc + 1) * 128, :],
                 waits=(("V", ws_free[par]),), sem=dsem)
        for c in range(NCH):
            bank = c
            tv = p.op("tensor", lambda e, wo=wo, c=c, bank=bank: e.transpose(
                out=P[bank][:, 0:128],
                in_=S["WS"][:, wo + c * 128:wo + (c + 1) * 128],
                identity=eyef[:, 0:128]),
                waits=((dsem, dv), ("V", ev_done.get((tc - 1, c), 0))), inc="T")
            ev_done[(tc, c)] = p.op("vector", lambda e, c=c, tc=tc, bank=bank:
                e.tensor_copy(out=S["R"][:, c * T + tc * 128:c * T + (tc + 1) * 128],
                              in_=P[bank][:, 0:128]),
                waits=(("T", tv),), inc="V")
        ws_free[par] = ev_done[(tc, NCH - 1)]
    ctx.r_ready_T = p.val("T")
    ctx.r_ready_V = p.val("V")
    ctx.r_ready_dG = ctx.plan.val("dG")


def emit_output_store(ctx, seq):
    p, S, P = ctx.plan, ctx.S, ctx.P
    p.barrier()
    eyef = S["CSTF"]
    last_dma = [0, 0]
    for tc in range(16):
        par = tc % 2
        wo = par * 1024
        evs = 0
        for c in range(NCH):
            bank = c
            tv = p.op("tensor", lambda e, c=c, tc=tc, bank=bank: e.transpose(
                out=P[bank][:, 0:128],
                in_=S["R"][:, c * T + tc * 128:c * T + (tc + 1) * 128],
                identity=eyef[:, 0:128]),
                waits=((("dO0" if par == 0 else "dO1"), last_dma[par]),
                       ("V", evs)), inc="T")
            evs = p.op("vector", lambda e, wo=wo, c=c, bank=bank: e.tensor_copy(
                out=S["WS"][:, wo + c * 128:wo + (c + 1) * 128],
                in_=P[bank][:, 0:128]), waits=(("T", tv),), inc="V")
        last_dma[par] = dma(ctx, ctx.D["out"][seq, tc * 128:(tc + 1) * 128, :],
                            S["WS"][:, wo:wo + 1024], waits=(("V", evs),),
                            sem=("dO0" if par == 0 else "dO1"))
    return p.val("dS")


def emit_attention(ctx, chunks, q_ap, k_slice_fn, vh_half, y_out_fn,
                   bias_ap=None, masks=None, first_wait=()):
    p, S, P = ctx.plan, ctx.S, ctx.P
    EB, VH = S["EB"], S["VH"]
    n = len(chunks)
    s_val = [0] * n
    e_val = [0] * n
    m_val = [0] * n
    av_val = [0] * n

    def emit_S(i):
        par = i % 2
        w = [("S", e_val[i - 2] if i >= 2 else ctx.sb_last[par])]
        if i == 0:
            w += list(first_wait)
        s_val[i] = p.op("tensor", lambda e, ci=chunks[i], par=par: e.matmul(
            out=P[par][:, :], lhsT=k_slice_fn(ci), rhs=q_ap,
            start=True, stop=True), waits=w, inc="T")

    emit_S(0)
    if n > 1:
        emit_S(1)
    for i in range(n):
        ci = chunks[i]
        par = i % 2
        if masks and ci in masks:
            m_val[i] = p.op("vector", lambda e, par=par, m=masks[ci]: e.tensor_tensor(
                out=P[par][:, :], in0=P[par][:, :], in1=m, op=ALU.add),
                waits=(("T", s_val[i]),), inc="V")
        w = [("T", s_val[i]),
             ("T", av_val[i - 2] if i >= 2 else ctx.eb_last[par])]
        if masks and ci in masks:
            w.append(("V", m_val[i]))
        kw = dict(bias=bias_ap, scale=1.0) if bias_ap is not None else {}
        e_val[i] = p.op("scalar", lambda e, par=par, kw=kw: e.activation(
            out=EB[:, par * 512:(par + 1) * 512], in_=P[par][:, :],
            func=AF.Exp, **kw), waits=w, inc="S")
        if i + 2 < n:
            emit_S(i + 2)
        wav = [("S", e_val[i])]
        if i == 0:
            wav.append(("V", ctx.av_last_V))
        av_val[i] = p.op("tensor", lambda e, ci=ci, par=par, i=i: e.matmul(
            out=P[2][0:65, :],
            lhsT=VH[:, vh_half * 1040 + ci * 65:vh_half * 1040 + ci * 65 + 65],
            rhs=EB[:, par * 512:(par + 1) * 512],
            start=(i == 0), stop=(i == n - 1)), waits=wav, inc="T")

    rz = p.op("vector", lp(ctx, lambda e: e.reciprocal(
        out=S["ROWB"][0:1, 512:1024], in_=P[2][64:65, :])),
        waits=(("T", av_val[n - 1]),), inc="V")
    rzb = p.op("tensor", lambda e: e.matmul(
        out=P[3][0:64, :], lhsT=S["CST"][0:1, 162:226],
        rhs=S["ROWB"][0:1, 512:1024], start=True, stop=True),
        waits=(("V", rz), ("V", ctx.rzb_last_V)), inc="T")
    ev = p.op("vector", lambda e: e.tensor_copy(
        out=S["TMP"][0:64, 0:512], in_=P[3][0:64, :]), waits=(("T", rzb),), inc="V")
    yv = y_out_fn(P[2], S["TMP"][0:64, 0:512], ev)
    ctx.rzb_last_V = ev
    ctx.av_last_V = yv
    ctx.last_av_T = av_val[n - 1]
    if n % 2 == 1:
        ctx.sb_last = [e_val[n - 1], e_val[n - 2] if n >= 2 else ctx.sb_last[1]]
        ctx.eb_last = [av_val[n - 1], av_val[n - 2] if n >= 2 else ctx.eb_last[1]]
    else:
        ctx.sb_last = [e_val[n - 2], e_val[n - 1]]
        ctx.eb_last = [av_val[n - 2], av_val[n - 1]]
    return yv


def emit_l0_attn(ctx, seq):
    p, S, P, W = ctx.plan, ctx.S, ctx.P, ctx.W
    UB, QY, KC, LAT, XT, VH, MSK = (S[k] for k in
        ("UB", "QY", "KC", "LAT", "XT", "VH", "MSK"))

    emit_ln(ctx, src="R", dst="UB")
    if L0A_LIMIT == "ln":
        return
    emit_spill_R(ctx)
    if L0A_LIMIT == "spill":
        emit_reload_R(ctx)
        ctx.plan.barrier()
        return

    dma(ctx, XT[:, 0:T], ctx.cap["l0a_t1"], sem="dC")
    ctx.xt_ready = dma(ctx, XT[:, T:2 * T], ctx.cap["l0a_t2"], sem="dC")
    for i, nm in enumerate(("l0a_qc", "l0a_qe", "l0a_qes", "l0a_kc", "l0a_ke",
                            "l0a_kes")):
        dma(ctx, S["WS"][0:64, i * 32:(i + 1) * 32], W[nm],
            waits=(("Wk", ctx.ws_last[0]),), sem="dC")
    smv = dma(ctx, S["WS"][0:64, 192:256], W["l0a_wvu"],
              waits=(("Wk", ctx.ws_last[0]),), sem="dC")
    UPQ1, UPQ2, UPK1, UPK2, VDUP = 4096, 4224, 4352, 4480, 4608
    g0 = p.op("gpsimd", lambda e: e.memset(XT[:, UPQ1:VDUP + 64], 0.0),
              waits=(("dC", smv),))
    cps = [(UPQ1 + 0, 0, 0, 0.125), (UPQ1 + 32, 32, 0, 0.125),
           (UPQ1 + 64, 0, 64, 0.125), (UPQ1 + 96, 32, 64, 0.125),
           (UPQ2 + 32, 64, 0, 0.125), (UPQ2 + 96, 64, 64, 0.125),
           (UPK1 + 0, 96, 0, None), (UPK1 + 32, 128, 0, None),
           (UPK1 + 64, 96, 64, None), (UPK1 + 96, 128, 64, None),
           (UPK2 + 32, 160, 0, None), (UPK2 + 96, 160, 64, None)]
    gv = 0
    for (dc, sc, dr, scale) in cps:
        if scale is None:
            gv = p.op("gpsimd", lambda e, dc=dc, sc=sc, dr=dr: e.tensor_copy(
                out=XT[dr:dr + 64, dc:dc + 32], in_=S["WS"][0:64, sc:sc + 32]),
                inc="G")
        else:
            gv = p.op("gpsimd", lambda e, dc=dc, sc=sc, dr=dr, s=scale:
                e.tensor_scalar_mul(out=XT[dr:dr + 64, dc:dc + 32],
                                    in0=S["WS"][0:64, sc:sc + 32], scalar1=s),
                inc="G")
    for dr in (0, 64):
        gv = p.op("gpsimd", lambda e, dr=dr: e.tensor_copy(
            out=XT[dr:dr + 64, VDUP:VDUP + 64], in_=S["WS"][0:64, 192:256]),
            inc="G")
    ctx.ws_last[0] = p.op("gpsimd", lambda e: None, inc="Wk")
    up_ready = gv
    vh_init = p.op("vector", lambda e: e.memset(VH[:, 0:2080], 1.0),
                   waits=(("T", ctx.last_av_T),), inc="V")
    ctx.av_last_V = max(ctx.av_last_V, vh_init)

    wq = load_matrix(ctx, W["l0a_wq"], NCH, 1024, S["WA"])
    lnv, lng = ctx.ln_done["V"], ctx.ln_done["G"]
    if L0A_LIMIT == "tables":
        emit_reload_R(ctx)
        ctx.plan.barrier()
        return

    def proj_pair(wbuf, wready, mc, dst, upo1, upo2):
        lat_ev = [0] * NTT
        for tt in range(NTT):
            pb = 4 + tt % 2
            tv = 0
            for kc in range(NCH):
                w = [("Wk", wready[kc])]
                if kc == 0:
                    w += [("V", lnv), ("G", lng), ("V", ctx.pb45_free[tt % 2])]
                tv = p.op("tensor", lambda e, mc=mc, tt=tt, kc=kc, pb=pb, wb=wbuf:
                     e.matmul(out=P[pb][:, :],
                              lhsT=wb[:, kc * 1024 + mc * 128:kc * 1024 + (mc + 1) * 128],
                              rhs=ub_ap(ctx, kc, tt * 512, 512),
                              start=(kc == 0), stop=(kc == NCH - 1)), waits=w,
                     inc="T" if kc == NCH - 1 else None)
            lat_ev[tt] = p.op("vector", lambda e, tt=tt, pb=pb:
                e.tensor_copy(out=LAT[:, tt * 512:(tt + 1) * 512], in_=P[pb][:, :]),
                waits=(("T", tv), ("T", ctx.lat_free[tt]),
                       ("dG", ctx.r_spilled)), inc="V")
            ctx.pb45_free[tt % 2] = lat_ev[tt]
        last_up = 0
        for tt in range(NTT):
            t1v = p.op("tensor", lambda e, tt=tt, o=upo1: e.matmul(
                out=P[6][:, :], lhsT=XT[:, o:o + 128],
                rhs=LAT[:, tt * 512:(tt + 1) * 512], start=True, stop=True),
                waits=(("V", lat_ev[tt]), ("G", up_ready), ("dC", ctx.xt_ready),
                       ("V", ctx.up_free[0])), inc="T")
            t2v = p.op("tensor", lambda e, tt=tt, o=upo2: e.matmul(
                out=P[7][:, :], lhsT=XT[:, o:o + 128],
                rhs=LAT[:, tt * 512:(tt + 1) * 512], start=True, stop=True),
                waits=(("V", ctx.up_free[1]),), inc="T")
            ctx.lat_free[tt] = t2v
            ctx.up_free[0] = p.op("vector", lambda e, tt=tt: e.tensor_tensor(
                out=S["TMP"][:, 0:512], in0=P[6][:, :],
                in1=XT[:, tt * 512:(tt + 1) * 512], op=ALU.mult),
                waits=(("T", t1v),), inc="V")
            ctx.up_free[1] = p.op("vector", lambda e, tt=tt: e.tensor_tensor(
                out=S["TMP2"][:, 0:512], in0=P[7][:, :],
                in1=XT[:, T + tt * 512:T + (tt + 1) * 512], op=ALU.mult),
                waits=(("T", t2v),), inc="V")
            last_up = p.op("vector", lambda e, mc=mc, tt=tt, d=dst: e.tensor_tensor(
                out=d[:, mc * T + tt * 512:mc * T + (tt + 1) * 512],
                in0=S["TMP"][:, 0:512], in1=S["TMP2"][:, 0:512], op=ALU.add),
                inc="V")
        return last_up

    ctx.lat_free = [0] * NTT
    ctx.up_free = [0, 0]
    ctx.pb45_free = [0, 0]
    qh_done = [0] * NCH
    for mc in range(NCH):
        qh_done[mc] = proj_pair(S["WA"], wq, mc, QY, UPQ1, UPQ2)
    if L0A_LIMIT == "q":
        emit_reload_R(ctx)
        ctx.plan.barrier()
        return
    wk = load_matrix(ctx, W["l0a_wk"], NCH, 1024, S["WA"], dst_free_T=p.val("T"))
    kh_done = [0] * NCH
    for mc in range(NCH):
        kh_done[mc] = proj_pair(S["WA"], wk, mc, KC, UPK1, UPK2)
    if L0A_LIMIT == "k":
        emit_reload_R(ctx)
        ctx.plan.barrier()
        return
    wv = load_matrix(ctx, W["l0a_wv"], NCH, 1024, S["WA"], dst_free_T=p.val("T"))

    y_done = 0
    for mc in range(NCH):
        lat_ev = [0] * NTT
        for tt in range(NTT):
            pb = 4 + tt % 2
            tv = 0
            for kc in range(NCH):
                w = [("Wk", wv[kc])]
                if kc == 0:
                    w.append(("V", ctx.pb45_free[tt % 2]))
                tv = p.op("tensor", lambda e, mc=mc, tt=tt, kc=kc, pb=pb:
                     e.matmul(out=P[pb][:, :],
                              lhsT=S["WA"][:, kc * 1024 + mc * 128:kc * 1024 + (mc + 1) * 128],
                              rhs=ub_ap(ctx, kc, tt * 512, 512),
                              start=(kc == 0), stop=(kc == NCH - 1)), waits=w,
                     inc="T" if kc == NCH - 1 else None)
            lat_ev[tt] = p.op("vector", lambda e, tt=tt, pb=pb: e.tensor_copy(
                out=LAT[:, tt * 512:(tt + 1) * 512], in_=P[pb][:, :]),
                waits=(("T", tv), ("T", ctx.lat_free[tt]),
                       ("dG", ctx.r_spilled)), inc="V")
            ctx.pb45_free[tt % 2] = lat_ev[tt]
        for hh in range(2):
            for grp in range(2):
                pb = 6 + grp
                tv = 0
                for ts8 in range(8):
                    ts = grp * 8 + ts8
                    w = [("V", lat_ev[ts // 4])]
                    if ts8 == 0:
                        w.append(("V", ctx.up_free[grp]))
                        w.append(("T", ctx.vh_last[hh]))
                    tv = p.op("tensor", lambda e, hh=hh, ts=ts, ts8=ts8, pb=pb:
                         e.matmul(out=P[pb][:, ts8 * 64:(ts8 + 1) * 64],
                                  lhsT=LAT[hh * 64:(hh + 1) * 64,
                                           ts * 128:(ts + 1) * 128],
                                  rhs=XT[hh * 64:(hh + 1) * 64, VDUP:VDUP + 64],
                                  start=True, stop=True), waits=w,
                         inc="T" if ts8 == 7 else None)
                for tt in range(NTT):
                    ctx.lat_free[tt] = tv
                ctx.up_free[grp] = p.op("vector", lambda e, hh=hh, grp=grp, pb=pb:
                    e.tensor_copy(
                        out=VH[:, hh * 1040 + grp * 520:hh * 1040 + (grp + 1) * 520]
                            .rearrange("p (g s) -> p g s", s=65)[:, :, 0:64],
                        in_=P[pb][:, 0:512]), waits=(("T", tv),), inc="V")
        vh_ready = ctx.up_free[1]
        if L0A_LIMIT == "v":
            continue
        for hh in range(2):
            h = 2 * mc + hh
            for tt in range(NTT):
                chunks = list(range(4 * tt + 4))
                masks = {}
                for ci in range(4 * tt, 4 * tt + 4):
                    st = 384 - 128 * (ci - 4 * tt)
                    masks[ci] = MSK[:, st:st + 512]
                q_ap = QY[hh * 64:(hh + 1) * 64, mc * T + tt * 512:mc * T + (tt + 1) * 512]

                def ksl(ci, hh=hh, mc=mc):
                    return KC[hh * 64:(hh + 1) * 64, mc * T + ci * 128:mc * T + (ci + 1) * 128]

                def yout(av_psum, rzb_sb, ev, hh=hh, mc=mc, tt=tt):
                    return p.op("vector", lambda e: e.tensor_tensor(
                        out=QY[hh * 64:(hh + 1) * 64,
                               mc * T + tt * 512:mc * T + (tt + 1) * 512],
                        in0=av_psum[0:64, :], in1=rzb_sb, op=ALU.mult),
                        waits=(("V", ev),), inc="V")

                base_wait = max(qh_done[mc], kh_done[mc], vh_ready)
                y_done = emit_attention(ctx, chunks, q_ap, ksl, hh, yout,
                                        masks=masks,
                                        first_wait=(("V", base_wait),))
            ctx.vh_last[hh] = ctx.last_av_T

    if L0A_LIMIT in ("v", "attn"):
        emit_reload_R(ctx)
        ctx.plan.barrier()
        return
    wc = load_matrix(ctx, W["l0a_wc"], NCH, 1024, S["WA"], dst_free_T=p.val("T"))
    emit_reload_R(ctx)
    ev_free = [0, 0]
    for tt in range(NTT):
        for mc in range(NCH):
            pb = 4 + mc % 2
            tv = 0
            for kc in range(NCH):
                w = [("Wk", wc[kc])]
                if kc == 0:
                    w += [("V", p.val("V") if (tt == 0 and mc == 0) else 0),
                          ("V", ev_free[mc % 2])]
                tv = p.op("tensor", lambda e, tt=tt, mc=mc, kc=kc, pb=pb: e.matmul(
                    out=P[pb][:, :],
                    lhsT=S["WA"][:, kc * 1024 + mc * 128:kc * 1024 + (mc + 1) * 128],
                    rhs=QY[:, kc * T + tt * 512:kc * T + (tt + 1) * 512],
                    start=(kc == 0), stop=(kc == NCH - 1)), waits=w,
                    inc="T" if kc == NCH - 1 else None)
            ev_free[mc % 2] = p.op("vector", lambda e, tt=tt, mc=mc, pb=pb:
                e.tensor_tensor(
                    out=S["R"][:, mc * T + tt * 512:mc * T + (tt + 1) * 512],
                    in0=P[pb][:, :],
                    in1=S["R"][:, mc * T + tt * 512:mc * T + (tt + 1) * 512],
                    op=ALU.add),
                waits=(("T", tv), ("dG", ctx.r_ready_dG)), inc="V")


BUILD_BLOCKS = ("l0a", "l0f", "l1a", "l1f")


def build_program(wshapes, blocks=None, taps=()):
    from contextlib import ExitStack
    blocks = BUILD_BLOCKS if blocks is None else blocks
    nc = bass.Bass("TRN2", target_bir_lowering=False, dynamic_dma_scratch_size=1024,
                   detect_race_conditions=DETECT_RACES)
    with ExitStack() as stack:
        ctx = build_ctx(nc, stack)
        for name, shp in wshapes.items():
            ctx.din(name, shp)
        ctx.W = ctx.D
        load_consts(ctx)
        tap_aps = {}
        for tn in taps:
            tap_aps[tn] = nc.dram_tensor(
                "tap_" + tn, [2, 128, NCH * T], F32, kind="ExternalOutput").ap()
        if DBG_B:
            ctx.dbgB = nc.dram_tensor(
                "dbgB", [2, 128, 4096], F32, kind="ExternalOutput").ap()
            ctx.dbgQ = nc.dram_tensor(
                "dbgQ", [2, 128, 2 * T], BF16, kind="ExternalOutput").ap()
            ctx.dbgX = nc.dram_tensor(
                "dbgX", [2, 128, 8192], BF16, kind="ExternalOutput").ap()
            ctx.dbgT = nc.dram_tensor(
                "dbgT", [2, 128, 512], F32, kind="ExternalOutput").ap()

        def tap(name, seq):
            if name in tap_aps:
                ctx.plan.barrier()
                dma(ctx, tap_aps[name][seq], ctx.S["R"][:, :])

        p = ctx.plan
        for seq in range(2):
            emit_input_load(ctx, seq)
            if "l0a" in blocks:
                emit_l0_attn(ctx, seq)
                tap("l0a", seq)
            if "l0f" in blocks:
                emit_l0_ffn(ctx, seq)
                tap("l0f", seq)
            if "l1a" in blocks:
                emit_l1_attn(ctx, seq)
                tap("l1a", seq)
            if "l1f" in blocks:
                emit_l1_ffn(ctx, seq)
                tap("l1f", seq)
            emit_output_store(ctx, seq)
        p.op("sync", lambda e: None,
             waits=tuple((s, p.val(s)) for s in list(p.cnt)
                         if s.startswith("d")))
        p.emit(nc)
    build_program.last_ctx = ctx
    return nc


def _shared_and_percore(inputs):
    inputs = {k_: v for k_, v in inputs.items() if k_ != "L1_ffn_gate_w"}
    w = _host_prep(inputs)
    w1 = w.pop("l1f_w1")
    w2 = w.pop("l1f_w2")
    b1 = w.pop("l1f_b1")
    b2 = w.pop("l1f_b2")
    x = np.ascontiguousarray(np.asarray(inputs["L0_attn_x"], dtype=np.float32))
    per_core = []
    for c in range(8):
        pc = {
            "x": np.ascontiguousarray(x[[c, c + 8]]),
            "l1f_w1e": np.ascontiguousarray(w1[c]),
            "l1f_w2e": np.ascontiguousarray(w2[c]),
            "l1f_b1ec": _col_layout(b1[c], 32),
            "l1f_b2ec": _col_layout(b2[c], NCH),
        }
        per_core.append(pc)
    return w, per_core


def kernel(**inputs):
    shared, per_core = _shared_and_percore(inputs)
    wshapes = {k: tuple(v.shape) for k, v in shared.items()}
    wshapes.update({k: tuple(v.shape) for k, v in per_core[0].items() if k != "x"})
    nc = build_program(wshapes, taps=tuple(DEBUG_TAPS))
    in_maps = []
    for c in range(8):
        m = dict(shared)
        m.update(per_core[c])
        in_maps.append(m)
    res = run_bass_kernel_spmd(nc, in_maps, core_ids=list(range(8)))
    out = np.zeros((16, T, DM), np.float32)
    for c in range(8):
        o = np.asarray(res.results[c]["out"])
        out[c] = o[0]
        out[c + 8] = o[1]
    if DEBUG_TAPS or DBG_B:
        kernel._tap_results = res.results
    return out


def simple_proj(ctx, wbuf, wready, dst, first_waits=()):
    p, S, P = ctx.plan, ctx.S, ctx.P
    done = [0] * NCH
    fw = list(first_waits)
    for mc in range(NCH):
        for tt in range(NTT):
            pb = 4 + tt % 2
            tv = 0
            for kc in range(NCH):
                w = [("Wk", wready[kc])]
                if kc == 0:
                    w += fw + [("V", ctx.pb45_free[tt % 2])]
                tv = p.op("tensor", lambda e, mc=mc, tt=tt, kc=kc, pb=pb, wb=wbuf:
                     e.matmul(out=P[pb][:, :],
                              lhsT=wb[:, kc * 1024 + mc * 128:kc * 1024 + (mc + 1) * 128],
                              rhs=ub_ap(ctx, kc, tt * 512, 512),
                              start=(kc == 0), stop=(kc == NCH - 1)), waits=w,
                     inc="T" if kc == NCH - 1 else None)
            done[mc] = p.op("vector", lambda e, mc=mc, tt=tt, pb=pb, d=dst:
                e.tensor_copy(out=d[:, mc * T + tt * 512:mc * T + (tt + 1) * 512],
                              in_=P[pb][:, :]),
                waits=(("T", tv), ("dG", ctx.r_spilled)), inc="V")
            ctx.pb45_free[tt % 2] = done[mc]
    return done


def emit_l1_attn(ctx, seq):
    p, S, P, W = ctx.plan, ctx.S, ctx.P, ctx.W
    UB, QY, KC, VH, MSK = (S[k] for k in ("UB", "QY", "KC", "VH", "MSK"))

    emit_ln(ctx, src="R", dst="UB")
    emit_spill_R(ctx)
    bo_dv = dma(ctx, S["COLS"][:, 0:8], W["l1a_boc"], sem="dC")

    ctx.pb45_free = [0, 0]
    lnv, lng = ctx.ln_done["V"], ctx.ln_done["G"]
    wq = load_matrix(ctx, W["l1a_wq"], NCH, 1024, S["WA"], scale=0.125,
                     dst_free_T=p.val("T"))
    qh = simple_proj(ctx, S["WA"], wq, QY, first_waits=[("V", lnv), ("G", lng)])
    wk = load_matrix(ctx, W["l1a_wk"], NCH, 1024, S["WA"], dst_free_T=p.val("T"))
    kh = simple_proj(ctx, S["WA"], wk, KC)
    wv = load_matrix(ctx, W["l1a_wv"], NCH, 1024, S["WA"], dst_free_T=p.val("T"))

    vh_init = p.op("vector", lambda e: e.memset(VH[:, 0:4160], 1.0),
                   waits=(("T", ctx.last_av_T),), inc="V")
    ctx.av_last_V = max(ctx.av_last_V, vh_init)
    y_done = 0
    for hb in range(4):
        ev = 0
        for ts in range(16):
            pb = 6 + ts % 2
            tv = 0
            for kc in range(NCH):
                w = [("Wk", wv[kc])]
                if kc == 0:
                    w.append(("V", ctx.p67_free[ts % 2]))
                    if ts < 2:
                        for sl in range(4):
                            w.append(("T", ctx.vh_last[sl]))
                tv = p.op("tensor", lambda e, ts=ts, kc=kc, pb=pb, hb=hb:
                     e.matmul(out=P[pb][:, 0:256],
                              lhsT=ub_ap(ctx, kc, ts * 128, 128),
                              rhs=S["WA"][:, kc * 1024 + hb * 256:kc * 1024 + hb * 256 + 256],
                              start=(kc == 0), stop=(kc == NCH - 1)), waits=w,
                     inc="T" if kc == NCH - 1 else None)
            ev = p.op("vector", lambda e, ts=ts, pb=pb: e.tensor_copy(
                out=VH[:, 0:4160].rearrange("a (s g) -> a s g", g=1040)
                    [:, :, ts * 65:ts * 65 + 64],
                in_=P[pb][:, 0:256].rearrange("a (s g) -> a s g", g=64)),
                waits=(("T", tv),), inc="V")
            ctx.p67_free[ts % 2] = ev
        for hq in range(4):
            h = 4 * hb + hq
            mc = h // 2
            hh = h % 2
            for tt in range(NTT):
                q_ap = QY[hh * 64:(hh + 1) * 64,
                          mc * T + tt * 512:mc * T + (tt + 1) * 512]

                def ksl(ci, hh=hh, mc=mc):
                    return KC[hh * 64:(hh + 1) * 64,
                              mc * T + ci * 128:mc * T + (ci + 1) * 128]

                def yout(av_psum, rzb_sb, evv, hh=hh, mc=mc, tt=tt):
                    return p.op("vector", lambda e: e.tensor_tensor(
                        out=QY[hh * 64:(hh + 1) * 64,
                               mc * T + tt * 512:mc * T + (tt + 1) * 512],
                        in0=av_psum[0:64, :], in1=rzb_sb, op=ALU.mult),
                        waits=(("V", evv),), inc="V")

                base_wait = max(qh[mc], kh[mc], ev)
                y_done = emit_attention(ctx, list(range(16)), q_ap, ksl, hq,
                                        yout, first_wait=(("V", base_wait),))
            ctx.vh_last[hq] = ctx.last_av_T

    wo = load_matrix(ctx, W["l1a_wo"], NCH, 1024, S["WA"], dst_free_T=p.val("T"))
    emit_reload_R(ctx)
    ev_free = [0, 0]
    for tt in range(NTT):
        for mc in range(NCH):
            pb = 4 + mc % 2
            tv = 0
            for kc in range(NCH):
                w = [("Wk", wo[kc])]
                if kc == 0:
                    w += [("V", p.val("V") if (tt == 0 and mc == 0) else 0),
                          ("V", ev_free[mc % 2])]
                tv = p.op("tensor", lambda e, tt=tt, mc=mc, kc=kc, pb=pb: e.matmul(
                    out=P[pb][:, :],
                    lhsT=S["WA"][:, kc * 1024 + mc * 128:kc * 1024 + (mc + 1) * 128],
                    rhs=QY[:, kc * T + tt * 512:kc * T + (tt + 1) * 512],
                    start=(kc == 0), stop=(kc == NCH - 1)), waits=w,
                    inc="T" if kc == NCH - 1 else None)
            ev_free[mc % 2] = p.op("vector", lambda e, tt=tt, mc=mc, pb=pb:
                e.scalar_tensor_tensor(
                    out=S["R"][:, mc * T + tt * 512:mc * T + (tt + 1) * 512],
                    in0=P[pb][:, :], scalar=S["COLS"][:, mc:mc + 1],
                    in1=S["R"][:, mc * T + tt * 512:mc * T + (tt + 1) * 512],
                    op0=ALU.add, op1=ALU.add),
                waits=(("T", tv), ("dG", ctx.r_ready_dG), ("dC", bo_dv)),
                inc="V")


def emit_l1_ffn(ctx, seq):
    p, S, P, W = ctx.plan, ctx.S, ctx.P, ctx.W
    UB, QY, COLS = S["UB"], S["QY"], S["COLS"]
    yb = S["QY"][:, 0:4096]

    emit_ln(ctx, src="R", dst="UB")
    cv = dma(ctx, COLS[:, 0:8], W["l1f_gc"], sem="dC")
    cv = dma(ctx, COLS[:, 8:16], W["l1f_bc"], sem="dC")
    cv = dma(ctx, COLS[:, 16:24], W["l1f_abc"], sem="dC")
    cv = dma(ctx, COLS[:, 24:32], W["l1f_b2ec"], sem="dC")
    cv = dma(ctx, COLS[:, 32:40], W["l1f_nbc"], sem="dC")
    cv = dma(ctx, COLS[:, 40:72], W["l1f_b1ec"], sem="dC")
    emit_ln(ctx, src="UB", dst="QY", gcol=COLS[:, 0:8], bcol=COLS[:, 8:16],
            sq_area="XT", col_dv=cv)

    wa_ = load_matrix(ctx, W["l1f_aw"], NCH, 1024, S["WA"], dst_free_T=p.val("T"))
    lnv, lng = ctx.ln_done["V"], ctx.ln_done["G"]
    ctx.pb45_free = [0, 0]
    xa_done = [0] * NCH
    for mc in range(NCH):
        for tt in range(NTT):
            pb = 4 + tt % 2
            tv = 0
            for kc in range(NCH):
                w = [("Wk", wa_[kc])]
                if kc == 0:
                    w += [("V", lnv), ("G", lng), ("V", ctx.pb45_free[tt % 2])]
                tv = p.op("tensor", lambda e, mc=mc, tt=tt, kc=kc, pb=pb:
                     e.matmul(out=P[pb][:, :],
                              lhsT=S["WA"][:, kc * 1024 + mc * 128:kc * 1024 + (mc + 1) * 128],
                              rhs=QY[:, kc * T + tt * 512:kc * T + (tt + 1) * 512],
                              start=(kc == 0), stop=(kc == NCH - 1)), waits=w,
                     inc="T" if kc == NCH - 1 else None)
            xa_done[mc] = p.op("vector", lambda e, mc=mc, tt=tt, pb=pb:
                e.scalar_tensor_tensor(
                    out=ub_ap(ctx, mc, tt * 512, 512), in0=P[pb][:, :],
                    scalar=COLS[:, 16 + mc:17 + mc],
                    in1=ub_ap(ctx, mc, tt * 512, 512),
                    op0=ALU.add, op1=ALU.add),
                waits=(("T", tv), ("dC", cv)), inc="V")
            ctx.pb45_free[tt % 2] = xa_done[mc]
    xa_all = max(xa_done)

    if L1F_LIMIT == "xa":
        ctx.plan.barrier()
        return
    nw = load_matrix(ctx, W["l1f_nw"], NCH, 1024, S["XT"], dst_free_T=p.val("T"))

    h_free = [0, 0]
    ysl_free = [0] * 8
    ynext_free = [0, 0]
    ctx.w1_slot_free = [p.val("T")] * 2
    for tt in range(NTT):
        hg_done = 0
        for sl in range(8):
            wslot = (sl % 2) * 4096
            w1v = load_matrix(ctx, W["l1f_w1e"][:, sl * 512:(sl + 1) * 512],
                              NCH, 512, S["WA"], dst_off=wslot,
                              dst_free_T=ctx.w1_slot_free[sl % 2])
            for s4 in range(4):
                s = sl * 4 + s4
                pb = 4 + s % 2
                tv = 0
                for kc in range(NCH):
                    w = [("Wk", w1v[kc])]
                    if kc == 0:
                        w += [("V", xa_all), ("S", h_free[s % 2])]
                    tv = p.op("tensor", lambda e, tt=tt, s=s, s4=s4, kc=kc,
                              pb=pb, wslot=wslot:
                         e.matmul(out=P[pb][:, :],
                                  lhsT=S["WA"][:, wslot + kc * 512 + s4 * 128:
                                               wslot + kc * 512 + (s4 + 1) * 128],
                                  rhs=ub_ap(ctx, kc, tt * 512, 512),
                                  start=(kc == 0), stop=(kc == NCH - 1)), waits=w,
                         inc="T" if kc == NCH - 1 else None)
                hg_done = p.op("scalar", lambda e, s=s, pb=pb: e.activation(
                    out=QY[:, s * 512:(s + 1) * 512], in_=P[pb][:, :],
                    func=AF.Gelu, bias=COLS[:, 40 + s:41 + s]),
                    waits=(("T", tv), ("T", ctx.qy_read_T), ("dC", cv)), inc="S")
                h_free[s % 2] = hg_done
            ctx.w1_slot_free[sl % 2] = p.val("T")
        ctx.w2_slot_free = [p.val("T")] * 8
        if L1F_LIMIT == "h1":
            ctx.plan.barrier()
            return
        yv = 0
        for s in range(32):
            w2slot = (s % 8) * 1024
            w2v = load_matrix(ctx, W["l1f_w2e"][s * 128:(s + 1) * 128, :],
                              1, 1024, S["WA"], dst_off=w2slot,
                              dst_free_T=ctx.w2_slot_free[s % 8])
            for mc in range(NCH):
                w = [("Wk", w2v[0])]
                if s == 0:
                    w += [("S", hg_done), ("V", ysl_free[mc])]
                yv = p.op("tensor", lambda e, s=s, mc=mc, w2slot=w2slot:
                     e.matmul(out=P[mc][:, :],
                              lhsT=S["WA"][:, w2slot + mc * 128:w2slot + (mc + 1) * 128],
                              rhs=QY[:, s * 512:(s + 1) * 512],
                              start=(s == 0), stop=(s == 31)), waits=w,
                     inc="T" if mc == NCH - 1 else None)
            ctx.w2_slot_free[s % 8] = yv
        yev = [0] * NCH
        for mc in range(NCH):
            yev[mc] = p.op("vector", lambda e, mc=mc: e.tensor_scalar_add(
                out=yb[:, mc * 512:(mc + 1) * 512], in0=P[mc][:, :],
                scalar1=COLS[:, 24 + mc:25 + mc]),
                waits=(("T", yv),), inc="V")
            ysl_free[mc] = yev[mc]
        if L1F_LIMIT == "y1":
            ctx.plan.barrier()
            return
        for mc in range(NCH):
            pb = 4 + mc % 2
            tv = 0
            for kc in range(NCH):
                w = [("Wk", nw[kc]), ("V", yev[kc])]
                if kc == 0:
                    w.append(("S", ynext_free[mc % 2]))
                    w.append(("V", yev[NCH - 1]))
                tv = p.op("tensor", lambda e, tt=tt, mc=mc, kc=kc, pb=pb:
                     e.matmul(out=P[pb][:, :],
                              lhsT=S["XT"][:, kc * 1024 + mc * 128:kc * 1024 + (mc + 1) * 128],
                              rhs=yb[:, kc * 512:(kc + 1) * 512],
                              start=(kc == 0), stop=(kc == NCH - 1)), waits=w,
                     inc="T" if kc == NCH - 1 else None)
            if L1F_LIMIT == "n2":
                sg = p.op("vector", lambda e, pb=pb: e.tensor_copy(
                    out=S["TMP2"][:, 0:512], in_=P[pb][:, :]),
                    waits=(("T", tv), ("V", ctx.t2_free)), inc="V")
                ynext_free[mc % 2] = 0
                ctx.t2_free = sg
                continue
            if L1F_LIMIT == "n3":
                sg = p.op("scalar", lambda e, pb=pb, mc=mc: e.activation(
                    out=S["TMP2"][:, 0:512], in_=P[pb][:, :], func=AF.Gelu,
                    bias=COLS[:, 32 + mc:33 + mc]),
                    waits=(("T", tv), ("V", ctx.t2_free)), inc="S")
                ynext_free[mc % 2] = sg
                ctx.t2_free = 0
                continue
            gb = 6 + mc % 2
            sg = p.op("scalar", lambda e, pb=pb, mc=mc, gb=gb: e.activation(
                out=P[gb][:, :], in_=P[pb][:, :], func=AF.Gelu,
                bias=COLS[:, 32 + mc:33 + mc]),
                waits=(("T", tv), ("V", ctx.t2_free2[mc % 2])), inc="S")
            ynext_free[mc % 2] = sg
            ctx.t2_free2[mc % 2] = p.op("vector",
                lambda e, mc=mc, tt=tt, gb=gb: e.tensor_tensor(
                    out=S["R"][:, mc * T + tt * 512:mc * T + (tt + 1) * 512],
                    in0=P[gb][:, :],
                    in1=S["R"][:, mc * T + tt * 512:mc * T + (tt + 1) * 512],
                    op=ALU.add), waits=(("S", sg),), inc="V")


def emit_head_mk(ctx, h, k_ready):
    p, S = ctx.plan, ctx.S
    QY = S["QY"]
    T2B = S["TMP2"][:, :].bitcast(BF16)
    dk = 4 + h // 2
    ret = 0
    for sg in range(16):
        c0 = max(0, (sg - 4) * 128)
        c1 = min(T, (sg + 2) * 128)
        ret = p.op("vector", lambda e, sg=sg, c0=c0, c1=c1: e.reduce_max(
            out=T2B[:, 600 + 2 * sg:601 + 2 * sg],
            in_=QY[:, dk * T + c0:dk * T + c1],
            axis=AX.X, apply_absolute_value=True),
            waits=(("V", k_ready), ("T", ctx.mkt_free)) if sg == 0 else (),
            inc="V" if sg == 15 else None)
    return ret


def emit_l0f_attention(ctx, h, w_, vh_half, q_ready, k_ready, v_ready, mk_ready):
    p, S, P = ctx.plan, ctx.S, ctx.P
    QY, KQB, MSK, CST, COLS, EB, VH = (S[k] for k in
        ("QY", "KQB", "MSK", "CST", "COLS", "EB", "VH"))
    hh = h % 2
    dq, dk = h // 2, 4 + h // 2
    k0 = max(0, (w_ - 1) * 512)
    klen = (w_ + 1) * 512 - k0
    cis = [ci for ci in range(8) if 4 * (w_ - 1) + ci >= 0]
    n = len(cis)
    base = [("V", max(q_ready, k_ready, v_ready))]

    if hh == 0:
        q_ap = QY[0:64, dq * T + w_ * 512:dq * T + (w_ + 1) * 512]
        k_ap = lambda g: QY[0:64, dk * T + g * 128:dk * T + (g + 1) * 128]
        kband = QY[0:64, dk * T + k0:dk * T + k0 + klen]
        cpv = 0
    else:
        cp1 = p.op("vector", lambda e: e.tensor_copy(
            out=KQB[0:64, 1024:1536],
            in_=QY[64:128, dq * T + w_ * 512:dq * T + (w_ + 1) * 512]),
            waits=base + [("T", ctx.kqb_free)], inc="V")
        cpv = p.op("vector", lambda e: e.tensor_copy(
            out=KQB[0:64, 0:klen],
            in_=QY[64:128, dk * T + k0:dk * T + k0 + klen]), inc="V")
        q_ap = KQB[0:64, 1024:1536]
        k_ap = lambda g: KQB[0:64, g * 128 - k0:(g + 1) * 128 - k0]
        kband = KQB[0:64, 0:klen]

    r0 = hh * 64
    T2B = S["TMP2"][:, :].bitcast(BF16)
    absq_src = QY[r0:r0 + 64, dq * T + w_ * 512:dq * T + (w_ + 1) * 512]
    av = p.op("scalar", lambda e: e.activation(
        out=T2B[r0:r0 + 64, 0:512], in_=absq_src, func=AF.Abs),
        waits=base + [("V", cpv), ("T", ctx.t2b_free),
                      ("V", getattr(ctx, "dbga_v", 0) if DBG_B else 0)], inc="S")
    brow = 0
    pos = dict(tile_position=(64, 0), skip_group_check=True) if r0 else {}
    for sl in range(4):
        sg = 4 * w_ + sl
        brow = p.op("tensor", lambda e, sl=sl, sg=sg: e.matmul(
            out=P[2][0:1, sl * 128:(sl + 1) * 128],
            lhsT=T2B[r0:r0 + 64, 600 + 2 * sg:601 + 2 * sg],
            rhs=T2B[r0:r0 + 64, sl * 128:(sl + 1) * 128],
            start=True, stop=True, **pos),
            waits=(("V", mk_ready), ("S", av),
                   ("V", ctx.av_last_V if sl == 0 else 0)), inc="T")
    brow_b = p.op("vector", lambda e: e.tensor_copy(
        out=S["ROWB"][0:1, 1024:1536], in_=P[2][0:1, :]),
        waits=(("T", brow), ("T", ctx.browb_free)), inc="V")
    ctx.brow_b_cur = brow_b
    if DBG_B and h == 0:
        XTF = S["XT"][:, :].bitcast(F32)
        p.op("vector", lambda e: e.tensor_copy(
            out=XTF[0:1, w_ * 512:(w_ + 1) * 512],
            in_=S["ROWB"][0:1, 1024:1536]))
        if w_ == 3:
            p.barrier()
            dma(ctx, ctx.dbgT[ctx.cur_seq], S["TMP2"][:, :])
            p.barrier()

    def mask_ap(ci):
        if ci <= 3:
            return MSK[:, 896 + 512 - 128 * ci:896 + 1024 - 128 * ci]
        return MSK[:, 1920 + 896 - 128 * ci:1920 + 1408 - 128 * ci]

    s_val = [0] * 16
    e_val = [0] * 16
    r_val = [0] * 16

    def emit_group(j):
        par = j % 2
        rb = j >= n
        ci = cis[j % n]
        g = 4 * (w_ - 1) + ci
        w = [("S", e_val[j - 2] if j >= 2 else ctx.sb_last[par])]
        if j == 0:
            w += base + [("V", cpv), ("S", av)]
        p.op("tensor", lambda e, g=g, par=par: e.matmul(
            out=P[par][:, :], lhsT=k_ap(g), rhs=q_ap,
            start=True, stop=False), waits=w)
        p.op("tensor", lambda e, ci=ci, par=par: e.matmul(
            out=P[par][:, :], lhsT=CST[:, 0:128], rhs=mask_ap(ci),
            start=False, stop=False))
        if rb:
            wm = [("V", ctx.mhat_v)] if j == n else []
            rrow = S["ROWB"][0:1, 0:512]
        else:
            wm = [("V", brow_b)] if j == 0 else []
            rrow = S["ROWB"][0:1, 1024:1536]
        s_val[j] = p.op("tensor", lambda e, par=par, rrow=rrow: e.matmul(
            out=P[par][:, :], lhsT=CST[0:1, 290:418],
            rhs=rrow, start=False, stop=True), waits=wm, inc="T")

    def emit_exp(j):
        par = j % 2
        rb = j >= n
        w = [("T", s_val[j]),
             ("T", r_val[j - 2] if j >= 2 else ctx.eb_last[par])]
        kw = dict(scale=0.1) if not rb else {}
        e_val[j] = p.op("scalar", lambda e, par=par, kw=kw: e.activation(
            out=EB[:, par * 512:(par + 1) * 512], in_=P[par][:, :],
            func=AF.Exp, **kw), waits=w, inc="S")

    def emit_reader(j):
        par = j % 2
        rb = j >= n
        ci = cis[j % n]
        g = 4 * (w_ - 1) + ci
        w = [("S", e_val[j])]
        if rb:
            if j == n:
                w.append(("V", ctx.brow_b_cur))
            r_val[j] = p.op("tensor", lambda e, g=g, par=par, j=j: e.matmul(
                out=P[2][0:65, :],
                lhsT=VH[:, vh_half * 1040 + g * 65:vh_half * 1040 + g * 65 + 65],
                rhs=EB[:, par * 512:(par + 1) * 512],
                start=(j == n), stop=(j == 2 * n - 1)), waits=w, inc="T")
        else:
            if j == 0:
                w.append(("S", ctx.z8_free))
                if DBG_B:
                    w.append(("V", getattr(ctx, "dbgz_v", 0)))
            r_val[j] = p.op("tensor", lambda e, par=par, j=j: e.matmul(
                out=P[3][0:1, :], lhsT=CST[:, 418:419],
                rhs=EB[:, par * 512:(par + 1) * 512],
                start=(j == 0), stop=(j == n - 1)), waits=w, inc="T")

    emit_group(0)
    if n > 1:
        emit_group(1)
    for j in range(n):
        emit_exp(j)
        if j + 2 < n:
            emit_group(j + 2)
        emit_reader(j)
    lnz = p.op("scalar", lambda e: e.activation(
        out=S["ROWF"][0:1, 0:512], in_=P[3][0:1, :], func=AF.Ln),
        waits=(("T", r_val[n - 1]), ("V", ctx.rowf_free)), inc="S")
    ctx.z8_free = lnz
    ctx.mhat_v = p.op("vector", lambda e: e.scalar_tensor_tensor(
        out=S["ROWB"][0:1, 0:512], in0=S["ROWF"][0:1, 0:512], scalar=10.0,
        in1=S["ROWB"][0:1, 1024:1536], op0=ALU.mult, op1=ALU.add),
        waits=(("S", lnz),), inc="V")
    if DBG_B and h == 0:
        XTF = S["XT"][:, :].bitcast(F32)
        p.op("vector", lambda e: e.tensor_copy(
            out=XTF[32:33, w_ * 512:(w_ + 1) * 512], in_=S["ROWB"][0:1, 0:512]))
        ctx.dbgz_v = p.op("vector", lambda e: e.tensor_copy(
            out=XTF[96:97, w_ * 512:(w_ + 1) * 512], in_=P[3][0:1, :]),
            waits=(("T", r_val[n - 1]),), inc="V")
    jb0 = n
    emit_group(jb0)
    if n > 1:
        emit_group(jb0 + 1)
    for j in range(jb0, jb0 + n):
        emit_exp(j)
        if j + 2 < jb0 + n:
            emit_group(j + 2)
        emit_reader(j)
    rz = p.op("vector", lp(ctx, lambda e: e.reciprocal(
        out=S["ROWB"][0:1, 512:1024], in_=P[2][64:65, :])),
        waits=(("T", r_val[jb0 + n - 1]),), inc="V")
    rzb = p.op("tensor", lambda e: e.matmul(
        out=P[6][0:64, :], lhsT=CST[0:1, 162:226],
        rhs=S["ROWB"][0:1, 512:1024], start=True, stop=True),
        waits=(("V", rz), ("V", ctx.rzb_last_V)), inc="T")
    ev2 = p.op("vector", lambda e: e.tensor_copy(
        out=S["TMP"][0:64, 0:512], in_=P[6][0:64, :]), waits=(("T", rzb),),
        inc="V")
    yv = p.op("vector", lambda e: e.tensor_tensor(
        out=QY[hh * 64:(hh + 1) * 64, dq * T + w_ * 512:dq * T + (w_ + 1) * 512],
        in0=P[2][0:64, :], in1=S["TMP"][0:64, 0:512], op=ALU.mult),
        waits=(("V", ev2),), inc="V")
    ctx.rzb_last_V = ev2
    ctx.av_last_V = yv
    ctx.rowf_free = ctx.mhat_v
    ctx.last_av_T = r_val[jb0 + n - 1]
    ctx.kqb_free = s_val[2 * n - 1]
    ctx.t2b_free = brow
    ctx.browb_free = s_val[n - 1]
    ctx.head_mk_last_T = brow
    if n % 2 == 1:
        ctx.sb_last = [e_val[jb0 + n - 1], e_val[jb0 + n - 2]]
        ctx.eb_last = [r_val[jb0 + n - 1], r_val[jb0 + n - 2]]
    else:
        ctx.sb_last = [e_val[jb0 + n - 2], e_val[jb0 + n - 1]]
        ctx.eb_last = [r_val[jb0 + n - 2], r_val[jb0 + n - 1]]
    return yv



def emit_l0_ffn(ctx, seq):
    ctx.cur_seq = seq
    p, S, P, W = ctx.plan, ctx.S, ctx.P, ctx.W
    UB, QY, VH, XT, MSK, COLS, CST, EB = (S[k] for k in
        ("UB", "QY", "VH", "XT", "MSK", "COLS", "CST", "EB"))
    F = S["R"]
    ones64 = CST[0:1, 162:226]
    ones128c = CST[:, 418:419]
    halfones = CST[:, 129:162]

    emit_ln(ctx, src="R", dst="UB", also_f32=True, spill_inside=True)

    for lay in range(L0F_LAYERS):
        cdv = dma(ctx, COLS[:, 0:1], W["l0f_qsc"][lay], sem="dC")
        cdv = dma(ctx, COLS[:, 1:2], W["l0f_qsc_sw"][lay], sem="dC")
        cdv = dma(ctx, COLS[:, 2:3], W["l0f_ksc"][lay], sem="dC")
        cdv = dma(ctx, COLS[:, 3:4], W["l0f_ksc_sw"][lay], sem="dC")
        cdv = dma(ctx, COLS[:, 4:12], W["l0f_gc"][lay], sem="dC")
        cdv = dma(ctx, COLS[:, 12:20], W["l0f_bc"][lay], sem="dC")
        cdv = dma(ctx, COLS[:, 20:28], W["l0f_b2c"][lay], sem="dC")
        cdv = dma(ctx, COLS[:, 28:50], W["l0f_b1ac"][lay], sem="dC")
        cdv = dma(ctx, COLS[:, 50:72], W["l0f_b1gc"][lay], sem="dC")
        tabv = 0
        for ti, (cname, coli) in enumerate((("xp_q1", 0), ("xp_q2", 1),
                                            ("xp_k1", 2), ("xp_k2", 3))):
            for half in range(2):
                par = (ti * 2 + half) % 2
                wo_ = par * 1024
                dv = dma(ctx, S["WS"][:, wo_:wo_ + 1024],
                         ctx.cap[cname][:, half * 1024:(half + 1) * 1024],
                         waits=(("Wk", ctx.ws_last[par]),),
                         sem="dW0" if par == 0 else "dW1")
                tabv = p.op("gpsimd", lambda e, ti=ti, half=half, wo_=wo_, coli=coli:
                    e.tensor_scalar_mul(
                        out=XT[:, ti * 2048 + half * 1024:ti * 2048 + (half + 1) * 1024],
                        in0=S["WS"][:, wo_:wo_ + 1024],
                        scalar1=COLS[:, coli:coli + 1]),
                    waits=(("dW0" if par == 0 else "dW1", dv), ("dC", cdv),
                           ("T", ctx.xt_free_T)), inc="G")
                ctx.ws_last[par] = p.op("gpsimd", lambda e: None, inc="Wk")

        lnv, lng = ctx.ln_done["V"], ctx.ln_done["G"]
        ctx.pb45_free = [0, 0]
        ctx.p67_free = [0, 0]
        qk_done = [0] * NCH
        for qk in range(2):
            wmain = load_matrix(ctx, W["l0f_wqkv"][lay][:, qk * 512:(qk + 1) * 512],
                                NCH, 512, S["WA"], dst_off=0,
                                dst_free_T=p.val("T"))
            wswap = load_matrix(ctx, W["l0f_wqkv_sw"][lay][:, qk * 512:(qk + 1) * 512],
                                NCH, 512, S["WA"], dst_off=4096,
                                dst_free_T=p.val("T"))
            for mc in range(4):
                dchunk = qk * 4 + mc
                for tt in range(NTT):
                    pbr = 4 + tt % 2
                    pbs = 6 + tt % 2
                    tvr = tvs = 0
                    for kc in range(NCH):
                        w = [("Wk", wmain[kc])]
                        if kc == 0:
                            w += [("V", lnv), ("G", lng),
                                  ("V", ctx.pb45_free[tt % 2])]
                        tvr = p.op("tensor", lambda e, mc=mc, tt=tt, kc=kc, pbr=pbr:
                             e.matmul(out=P[pbr][:, :],
                                      lhsT=S["WA"][:, kc * 512 + mc * 128:kc * 512 + (mc + 1) * 128],
                                      rhs=ub_ap(ctx, kc, tt * 512, 512, shift=(kc >= 4)),
                                      start=(kc == 0), stop=(kc == NCH - 1)),
                             waits=w, inc="T" if kc == NCH - 1 else None)
                    for kc in range(NCH):
                        w = [("Wk", wswap[kc])]
                        if kc == 0:
                            w.append(("V", ctx.p67_free[tt % 2]))
                        tvs = p.op("tensor", lambda e, mc=mc, tt=tt, kc=kc, pbs=pbs:
                             e.matmul(out=P[pbs][:, :],
                                      lhsT=S["WA"][:, 4096 + kc * 512 + mc * 128:4096 + kc * 512 + (mc + 1) * 128],
                                      rhs=ub_ap(ctx, kc, tt * 512, 512, shift=(kc >= 4)),
                                      start=(kc == 0), stop=(kc == NCH - 1)),
                             waits=w, inc="T" if kc == NCH - 1 else None)
                    sq = p.op("scalar", lambda e, pbr=pbr, par=tt % 2: e.activation(
                        out=EB[:, par * 512:(par + 1) * 512], in_=P[pbr][:, :],
                        func=AF.Square), waits=(("T", tvr), ("V", ctx.eb_qk_free[tt % 2])),
                        inc="S")
                    ssv = p.op("tensor", lambda e, par=tt % 2: e.matmul(
                        out=P[2][0:33, :], lhsT=halfones,
                        rhs=EB[:, par * 512:(par + 1) * 512], start=True, stop=True),
                        waits=(("S", sq), ("V", ctx.p2_free)), inc="T")
                    l1 = p.op("scalar", lambda e: e.activation(
                        out=S["ROWF"][0:33, 0:512], in_=P[2][0:33, :],
                        func=AF.Ln, bias=S["CSTE"][0:33, 0:1]),
                        waits=(("T", ssv), ("V", ctx.rowf_free)))
                    l2 = p.op("scalar", lambda e: e.activation(
                        out=S["ROWB"][0:1, 0:512], in_=S["ROWF"][0:1, 0:512],
                        func=AF.Exp, scale=-0.5), inc="S")
                    rc = p.op("vector", lambda e: e.tensor_copy(
                        out=S["ROWF"][0:1, 512:1024], in_=S["ROWF"][32:33, 0:512]),
                        waits=(("S", l2),), inc="V")
                    rc2 = p.op("scalar", lambda e: e.activation(
                        out=S["ROWB"][0:1, 512:1024], in_=S["ROWF"][0:1, 512:1024],
                        func=AF.Exp, scale=-0.5), waits=(("V", rc),), inc="S")
                    b1v = p.op("tensor", lambda e: e.matmul(
                        out=P[3][0:64, :], lhsT=ones64,
                        rhs=S["ROWB"][0:1, 0:512], start=True, stop=True,
                        tile_position=(0, 0), skip_group_check=True),
                        waits=(("S", rc2), ("V", ctx.p3_free)), inc="T")
                    b2v = p.op("tensor", lambda e: e.matmul(
                        out=P[3][64:128, :], lhsT=ones64,
                        rhs=S["ROWB"][0:1, 512:1024], start=True, stop=True,
                        tile_position=(0, 64), skip_group_check=True), inc="T")
                    toff = 0 if qk == 0 else 2 * 2048
                    t1v = p.op("vector", lambda e, pbr=pbr, tt=tt, toff=toff:
                        e.tensor_tensor(out=S["TMP"][:, 0:512], in0=P[pbr][:, :],
                                        in1=XT[:, toff + tt * 512:toff + (tt + 1) * 512],
                                        op=ALU.mult),
                        waits=(("T", b2v), ("G", tabv)))
                    ctx.pb45_free[tt % 2] = p.op("vector", lambda e: None, inc="V")
                    t2v = p.op("vector", lambda e, pbs=pbs, tt=tt, toff=toff:
                        e.tensor_tensor(out=S["TMP2"][:, 0:512], in0=P[pbs][:, :],
                                        in1=XT[:, toff + 2048 + tt * 512:toff + 2048 + (tt + 1) * 512],
                                        op=ALU.mult), waits=(("T", tvs),))
                    ctx.p67_free[tt % 2] = p.op("vector", lambda e: None, inc="V")
                    t3v = p.op("vector", lambda e: e.tensor_tensor(
                        out=S["TMP"][:, 0:512], in0=S["TMP"][:, 0:512],
                        in1=S["TMP2"][:, 0:512], op=ALU.add))
                    qk_done[dchunk] = p.op("vector", lambda e, dchunk=dchunk, tt=tt:
                        e.tensor_tensor(
                            out=QY[:, dchunk * T + tt * 512:dchunk * T + (tt + 1) * 512],
                            in0=S["TMP"][:, 0:512], in1=P[3][:, :], op=ALU.mult),
                        inc="V")
                    ctx.p3_free = qk_done[dchunk]
                    ctx.p2_free = qk_done[dchunk]
                    ctx.rowf_free = qk_done[dchunk]
                    ctx.eb_qk_free[tt % 2] = qk_done[dchunk]

        if DBG_B and lay == 0:
            p.barrier()
            dma(ctx, ctx.dbgQ[seq][:, 0:T], QY[:, 0:T])
            dma(ctx, ctx.dbgQ[seq][:, T:2 * T], QY[:, 4 * T:5 * T])
            dma(ctx, ctx.dbgX[seq], XT[:, :])
            p.barrier()

        wv = load_matrix(ctx, W["l0f_wqkv"][lay][:, 1024:1536], NCH, 512,
                         S["WA"], dst_off=0, dst_free_T=p.val("T"))
        wwo = load_matrix(ctx, W["l0f_wo"][lay], 4, 1024, S["WA"], dst_off=4096,
                          dst_free_T=p.val("T"))
        vh_init = p.op("vector", lambda e: e.memset(VH[:, 0:4160], 1.0),
                       waits=(("T", ctx.last_av_T),), inc="V")
        ctx.av_last_V = max(ctx.av_last_V, vh_init)
        y_done = 0
        for hb in range(2):
            ev = 0
            for ts in range(16):
                pb = 4 + ts % 2
                tv = 0
                for kc in range(NCH):
                    w = [("Wk", wv[kc])]
                    if kc == 0:
                        w.append(("V", ctx.pb45_free[ts % 2]))
                        if ts < 2:
                            for sl in range(4):
                                w.append(("T", ctx.vh_last[sl]))
                    tv = p.op("tensor", lambda e, ts=ts, kc=kc, pb=pb, hb=hb:
                         e.matmul(out=P[pb][:, 0:256],
                                  lhsT=ub_ap(ctx, kc, ts * 128, 128, shift=(kc >= 4)),
                                  rhs=S["WA"][:, kc * 512 + hb * 256:kc * 512 + hb * 256 + 256],
                                  start=(kc == 0), stop=(kc == NCH - 1)), waits=w,
                         inc="T" if kc == NCH - 1 else None)
                ev = p.op("vector", lambda e, ts=ts, pb=pb: e.tensor_copy(
                    out=VH[:, 0:4160].rearrange("a (s g) -> a s g", g=1040)
                        [:, :, ts * 65:ts * 65 + 64],
                    in_=P[pb][:, 0:256].rearrange("a (s g) -> a s g", g=64)),
                    waits=(("T", tv),), inc="V")
                ctx.pb45_free[ts % 2] = ev
            for hq in range(4):
                h = 4 * hb + hq
                if L0F_LIMIT == "attn1" and h > 0:
                    continue
                if hq % 2 == 0:
                    ctx.mkt_free = ctx.head_mk_last_T
                    mk_ready = emit_head_mk(ctx, h, qk_done[4 + h // 2])
                for w_ in range(4):
                    y_done = emit_l0f_attention(
                        ctx, h, w_, hq,
                        qk_done[h // 2], qk_done[4 + h // 2], ev, mk_ready)
                ctx.vh_last[hq] = ctx.last_av_T
        if L0F_LIMIT in ("attn", "attny", "attn1") and hb == 1:
            p.barrier()
            if DBG_B:
                dma(ctx, ctx.dbgB[seq], S["XT"][:, :].bitcast(F32))
                p.barrier()
            if L0F_LIMIT in ("attny", "attn1"):
                for c in range(NCH):
                    p.op("vector", lambda e, c=c: e.tensor_copy(
                        out=F[:, c * T:(c + 1) * T],
                        in_=QY[:, c * T:(c + 1) * T]), inc="V")
                p.barrier()
            return

        ev_free = [0, 0]
        for tt in range(NTT):
            for mcO in range(NCH):
                pb = 4 + mcO % 2
                tv = 0
                for kc in range(4):
                    w = [("Wk", wwo[kc])]
                    if kc == 0:
                        w += [("V", p.val("V") if (tt == 0 and mcO == 0) else 0),
                              ("V", ev_free[mcO % 2])]
                    tv = p.op("tensor", lambda e, tt=tt, mcO=mcO, kc=kc, pb=pb:
                         e.matmul(out=P[pb][:, :],
                                  lhsT=S["WA"][:, 4096 + kc * 1024 + mcO * 128:
                                               4096 + kc * 1024 + (mcO + 1) * 128],
                                  rhs=QY[:, kc * T + tt * 512:kc * T + (tt + 1) * 512],
                                  start=(kc == 0), stop=(kc == 3)), waits=w,
                         inc="T" if kc == 3 else None)
                ev_free[mcO % 2] = p.op("vector", lambda e, tt=tt, mcO=mcO, pb=pb:
                    e.tensor_tensor(
                        out=F[:, mcO * T + tt * 512:mcO * T + (tt + 1) * 512],
                        in0=P[pb][:, :],
                        in1=F[:, mcO * T + tt * 512:mcO * T + (tt + 1) * 512],
                        op=ALU.add), waits=(("T", tv),), inc="V")

        if L0F_LIMIT == "wo":
            p.barrier()
            return
        emit_ln(ctx, src="R", dst="UB", gcol=COLS[:, 4:12], bcol=COLS[:, 12:20],
                shifted=True, sq_area="QY", col_dv=p.val("dC"))
        lnv, lng = ctx.ln_done["V"], ctx.ln_done["G"]
        NA = [128] * 21 + [42]
        ctx.w1_slot_free = [p.val("T")] * 2
        glu_done = 0
        ctx.qy_rd = 0
        for tt in range(NTT):
            for s in range(22):
                na = NA[s]
                slot = s % 2
                wa_v = load_matrix(ctx, W["l0f_w1"][lay][:, s * 128:s * 128 + na],
                                   NCH, na, S["WA"], dst_off=slot * 2048,
                                   dst_free_T=ctx.w1_slot_free[slot])
                wg_v = load_matrix(ctx, W["l0f_w1"][lay][:, 2730 + s * 128:2730 + s * 128 + na],
                                   NCH, na, S["WA"], dst_off=slot * 2048 + 1024,
                                   dst_free_T=ctx.w1_slot_free[slot])
                pa = 4 + s % 2
                pg = 6 + s % 2
                tva = tvg = 0
                for kc in range(NCH):
                    w = [("Wk", wa_v[kc])]
                    if kc == 0:
                        w += [("V", lnv), ("G", lng), ("V", ctx.pb45_free[s % 2]),
                              ("V", ctx.ybank_free[pa])]
                    tva = p.op("tensor", lambda e, s=s, kc=kc, pa=pa, na=na,
                               slot=slot, tt=tt:
                         e.matmul(out=P[pa][0:na, :],
                                  lhsT=S["WA"][:, slot * 2048 + kc * na:slot * 2048 + (kc + 1) * na],
                                  rhs=ub_ap(ctx, kc, tt * 512, 512, shift=(kc >= 4)),
                                  start=(kc == 0), stop=(kc == NCH - 1)), waits=w,
                         inc="T" if kc == NCH - 1 else None)
                for kc in range(NCH):
                    w = [("Wk", wg_v[kc])]
                    if kc == 0:
                        w.append(("V", ctx.p67_free[s % 2]))
                        w.append(("V", ctx.ybank_free[pg]))
                    tvg = p.op("tensor", lambda e, s=s, kc=kc, pg=pg, na=na,
                               slot=slot, tt=tt:
                         e.matmul(out=P[pg][0:na, :],
                                  lhsT=S["WA"][:, slot * 2048 + 1024 + kc * na:slot * 2048 + 1024 + (kc + 1) * na],
                                  rhs=ub_ap(ctx, kc, tt * 512, 512, shift=(kc >= 4)),
                                  start=(kc == 0), stop=(kc == NCH - 1)), waits=w,
                         inc="T" if kc == NCH - 1 else None)
                ctx.w1_slot_free[slot] = tvg
                gg = p.op("scalar", lambda e, pg=pg, s=s, na=na, par=s % 2:
                    e.activation(out=EB[:, par * 512:(par + 1) * 512][0:na, :],
                                 in_=P[pg][0:na, :], func=AF.Gelu,
                                 bias=COLS[0:na, 50 + s:51 + s]),
                    waits=(("T", tvg), ("V", ctx.eb_qk_free[s % 2]),
                           ("dC", cdv)), inc="S")
                glu_done = p.op("vector", lambda e, pa=pa, s=s, na=na, par=s % 2:
                    e.scalar_tensor_tensor(
                        out=QY[:, s * 512:(s + 1) * 512][0:na, :],
                        in0=P[pa][0:na, :], scalar=COLS[0:na, 28 + s:29 + s],
                        in1=EB[:, par * 512:(par + 1) * 512][0:na, :],
                        op0=ALU.add, op1=ALU.mult),
                    waits=(("S", gg), ("T", ctx.qy_rd)), inc="V")
                ctx.eb_qk_free[s % 2] = glu_done
                ctx.pb45_free[s % 2] = glu_done
                ctx.p67_free[s % 2] = glu_done
            yv = 0
            ctx.ybank_free_S = p.val("S") if tt == 0 else 0
            ctx.w2_slot_free = [p.val("T")] * 2
            for s in range(22):
                na = NA[s]
                slot2 = s % 2
                w2v = load_matrix(ctx, W["l0f_w2"][lay][s * 128:s * 128 + na, :],
                                  1, 1024, S["WA"], dst_off=4096 + slot2 * 1024,
                                  dst_free_T=ctx.w2_slot_free[slot2])
                for mcO in range(NCH):
                    w = [("Wk", w2v[0])]
                    if s == 0:
                        w += [("V", glu_done), ("V", ctx.ybank_free[mcO]),
                              ("S", ctx.ybank_free_S)]
                    yv = p.op("tensor", lambda e, s=s, mcO=mcO, na=na, slot2=slot2:
                         e.matmul(out=P[mcO][:, :],
                                  lhsT=S["WA"][:, 4096 + slot2 * 1024 + mcO * 128:
                                               4096 + slot2 * 1024 + (mcO + 1) * 128][0:na, :],
                                  rhs=QY[:, s * 512:(s + 1) * 512][0:na, :],
                                  start=(s == 0), stop=(s == 21)), waits=w,
                         inc="T" if mcO == NCH - 1 else None)
                ctx.w2_slot_free[slot2] = yv
            for mcO in range(NCH):
                ctx.ybank_free[mcO] = p.op("vector",
                    lambda e, mcO=mcO, tt=tt: e.scalar_tensor_tensor(
                        out=F[:, mcO * T + tt * 512:mcO * T + (tt + 1) * 512],
                        in0=P[mcO][:, :], scalar=COLS[:, 20 + mcO:21 + mcO],
                        in1=F[:, mcO * T + tt * 512:mcO * T + (tt + 1) * 512],
                        op0=ALU.add, op1=ALU.add),
                    waits=(("T", yv), ("dC", cdv)), inc="V")
            ctx.qy_rd = p.val("T")
            ctx.w1_slot_free = [p.val("T")] * 2
        if lay == 0:
            p.barrier()
            for c in range(NCH):
                eng = "vector" if c % 2 == 0 else "gpsimd"
                sem = "V" if eng == "vector" else "G"
                p.op(eng, lambda e, c=c: e.memset(S["UB"][:, c * TP:c * TP + PAD], 0.0))
                ctx.ln_done[sem] = p.op(
                    eng, lambda e, c=c: e.tensor_copy(
                        out=S["UB"][:, c * TP + PAD:c * TP + PAD + T],
                        in_=F[:, c * T:(c + 1) * T]), inc=sem)
        ctx.xt_free_T = p.val("T")

    p.barrier()
    radd = 0
    for c in range(NCH):
        dv = dma(ctx, S["WS"][:, 0:T], ctx.D["r_spill"][:, c * T:(c + 1) * T],
                 sem="dG", waits=(("V", radd),))
        radd = p.op("vector", lambda e, c=c: e.tensor_tensor(
            out=F[:, c * T:(c + 1) * T], in0=S["WS"][:, 0:T],
            in1=F[:, c * T:(c + 1) * T], op=ALU.add),
            waits=(("dG", dv),), inc="V")
    p.barrier()
